# version 01 (final)
# baseline (speedup 1.0000x reference)
import sys as _sys
import numpy as np
import ml_dtypes
from contextlib import ExitStack

import concourse.bass as bass
import concourse.mybir as mybir
from concourse.ap import AP as _APCLS
from concourse.bass_utils import run_bass_kernel_spmd


def _ap(x):
    return x if isinstance(x, _APCLS) else x[:]

F32 = mybir.dt.float32
BF16 = mybir.dt.bfloat16
I32 = mybir.dt.int32
AF = mybir.ActivationFunctionType
ALU = mybir.AluOpType
AX = mybir.AxisListType
BF = ml_dtypes.bfloat16

S = 4
T = 2048
C = 1024
CC = 8
TW = 512
NTC = T // TW
RH = 16
AH = 8
HD = 64
INNER = 512
FPAD = 2816
FC = FPAD // 128
MAGIC = 0x5F3759DF

DEBUG_TAPS = []
PHASE_LIMIT = 99

ENGS = ("tensor", "vector", "scalar", "gpsimd", "sync")
SKIP_SAME = {"tensor"}
SHUF32 = [8, 9, 10, 11, 12, 13, 14, 15, 0, 1, 2, 3, 4, 5, 6, 7,
          24, 25, 26, 27, 28, 29, 30, 31, 16, 17, 18, 19, 20, 21, 22, 23]


DMA_SLOTS = 6


class Prog:

    def __init__(self, nc):
        self.nc = nc
        self.semnames = list(ENGS) + [f"dq_{q}_{i}" for q in ("sync", "gpsimd")
                                      for i in range(DMA_SLOTS)]
        self.sems = {}
        self.recs = {e: [] for e in ENGS}
        self.ptr = {e: 0 for e in ENGS}
        self.cnt = {sn: 0 for sn in self.semnames}
        self.waited = {e: {sn: 0 for sn in self.semnames} for e in ENGS}
        self.dma_idx = {"sync": 0, "gpsimd": 0}
        self.writer = {}
        self.readers = {}
        self.log = None

    def _wait(self, waits, e, sn, val):
        if val <= 0 or self.waited[e][sn] >= val:
            return
        self.waited[e][sn] = val
        waits.append((sn, val))

    def op(self, e, fn, reads=(), writes=(), dma=False):
        waits = []
        for k in reads:
            w = self.writer.get(k)
            if w and (w[0] != e or e not in SKIP_SAME):
                self._wait(waits, e, w[0], w[1])
        for k in writes:
            w = self.writer.get(k)
            if w and (w[0] != e or e not in SKIP_SAME):
                self._wait(waits, e, w[0], w[1])
            for (rsn, rc) in self.readers.get(k, ()):
                if rsn != e or e not in SKIP_SAME:
                    self._wait(waits, e, rsn, rc)
        if dma:
            slot = self.dma_idx[e] % DMA_SLOTS
            self.dma_idx[e] += 1
            sn = f"dq_{e}_{slot}"
            self._wait(waits, e, sn, self.cnt[sn])
            self.cnt[sn] += 16
            ca = self.cnt[sn]
        else:
            sn = e
            self.cnt[e] += 1
            ca = self.cnt[e]
        for k in reads:
            self.readers.setdefault(k, []).append((sn, ca))
        for k in writes:
            self.writer[k] = (sn, ca)
            self.readers[k] = []
        self.recs[e].append((fn, waits, sn if dma else None))
        if self.log is not None:
            ln = _sys._getframe(2).f_lineno
            self.log.append((e, sn, ca, ln, list(waits), list(reads), list(writes)))

    def barrier(self):
        for e in ENGS:
            waits = []
            for sn in self.semnames:
                if sn != e:
                    self._wait(waits, e, sn, self.cnt[sn])
            if waits:
                self.recs[e].append((None, waits, None))
        self.writer.clear()
        self.readers.clear()

    def alloc_sems(self, stack):
        for sn in self.semnames:
            self.sems[sn] = stack.enter_context(self.nc.semaphore(f"sem_{sn}"))

    def emit_phase(self):
        sems = self.sems
        with self.nc.Block() as blk:
            def runner(e):
                lo, hi = self.ptr[e], len(self.recs[e])
                chunk = self.recs[e][lo:hi]
                self.ptr[e] = hi

                def run(E):
                    for (fn, waits, dma_sn) in chunk:
                        for (sn, val) in waits:
                            E.wait_ge(sems[sn], val)
                        if fn is not None:
                            inst = fn(E)
                            if dma_sn is not None:
                                inst.then_inc(sems[dma_sn], 16)
                            else:
                                inst.then_inc(sems[e], 1)
                return run
            blk.tensor(runner("tensor"))
            blk.vector(runner("vector"))
            blk.scalar(runner("scalar"))
            blk.gpsimd(runner("gpsimd"))
            blk.sync(runner("sync"))


def host_consts():
    c = {}
    c["ident_f32"] = np.eye(128, dtype=np.float32)
    c["ones_1x128_f32"] = np.ones((1, 128), np.float32)
    c["ones_1x128_bf16"] = np.ones((1, 128), BF)
    c["ones_1x64_f32"] = np.ones((1, 64), np.float32)
    c["ones_1x16_f32"] = np.ones((1, 16), np.float32)
    c["ones_128x1_f32"] = np.ones((128, 1), np.float32)
    c["magic_row"] = np.full((16, 512), MAGIC, np.int32)

    ind = np.zeros((16, 128), np.float32)
    for h in range(16):
        ind[h, (h % 2) * 64:(h % 2) * 64 + 64] = 1.0
    c["ind16"] = ind
    bc = np.zeros((16, 8), np.float32)
    for h in range(16):
        bc[h, h // 2] = 1.0
    c["bconst"] = bc
    selbeu = np.zeros((128, 8 * 48), np.float32)
    selw = np.zeros((16, 8 * 128), np.float32)
    for ic in range(8):
        for p in range(128):
            h = 2 * ic + p // 64
            selbeu[p, ic * 48 + h] = 1.0
            selbeu[p, ic * 48 + 32 + h] = 1.0
            selw[h, ic * 128 + p] = 1.0
    c["selbeu"] = selbeu
    c["selw"] = selw

    sel_ssq = np.zeros((128, 8 * 16), np.float32)
    sel_rn = np.zeros((16, 8 * 128), np.float32)
    for i in range(8):
        base = 0 if i < 4 else 8
        oc = i % 4
        for p in range(128):
            r = base + 2 * oc + p // 64
            sel_ssq[p, i * 16 + r] = 1.0
            sel_rn[r, i * 128 + p] = 1.0
    c["sel_ssq"] = sel_ssq
    c["sel_rn"] = sel_rn

    n = T
    half = np.arange(0, HD, 2, dtype=np.float32)
    inv_freq = 1.0 / (10000.0 ** (half / HD))
    t = np.arange(n, dtype=np.float32)
    freqs = t[:, None] * inv_freq[None, :]
    freqs = np.concatenate([freqs, freqs], -1)
    base = (half + 0.4 * HD) / (1.4 * HD)
    power = (t - n // 2) / 256.0
    scale = base[None, :] ** power[:, None]
    scale = np.concatenate([scale, scale], -1)
    cos, sin = np.cos(freqs), np.sin(freqs)
    sgn = np.ones((HD, 1), np.float32)
    sgn[32:] = -1.0
    c["tab_qc"] = np.tile((8.0 * cos * scale).T, (2, 1)).astype(np.float32)
    c["tab_qs"] = np.tile((8.0 * sin * scale).T * sgn, (2, 1)).astype(np.float32)
    c["tab_kc"] = np.tile((cos / scale).T, (2, 1)).astype(np.float32)
    c["tab_ks"] = np.tile((sin / scale).T * sgn, (2, 1)).astype(np.float32)

    ii = np.arange(128)[:, None]
    jj = np.arange(128)[None, :]
    c["mask_low"] = np.where(jj < ii, np.float32(-1e9), np.float32(0)).astype(np.float32)
    c["mask_high"] = np.where(jj > ii, np.float32(-1e9), np.float32(0)).astype(np.float32)
    c["t_ge"] = (jj >= ii).T.astype(BF)
    c["t_ge"] = (np.arange(128)[None, :] >= np.arange(128)[:, None]).astype(BF)
    c["t_le"] = (np.arange(128)[None, :] <= np.arange(128)[:, None]).astype(BF)
    return c


def build_program(taps=(), phase_limit=99):
    nc = bass.Bass("TRN2", target_bir_lowering=False)
    P = Prog(nc)
    HC = host_consts()
    dk = lambda name: "ExternalOutput" if name in taps else "Internal"

    x_in = nc.dram_tensor("x", [S, T, C], F32, kind="ExternalInput")
    out_d = nc.dram_tensor("out", [S, T, C], F32, kind="ExternalOutput")
    wkT = nc.dram_tensor("wkT", [C, C], F32, kind="ExternalInput")
    wvT = nc.dram_tensor("wvT", [C, C], F32, kind="ExternalInput")
    wrT = nc.dram_tensor("wrT", [C, C], F32, kind="ExternalInput")
    woT = nc.dram_tensor("woT", [C, C], F32, kind="ExternalInput")
    mixd = {m: nc.dram_tensor(m, [C], F32, kind="ExternalInput") for m in ("mixk", "mixv", "mixr")}
    td_d = nc.dram_tensor("td", [RH], F32, kind="ExternalInput")
    tf_d = nc.dram_tensor("tf", [RH], F32, kind="ExternalInput")
    wqkv_d = nc.dram_tensor("wqkv", [2, C, 1536], F32, kind="ExternalInput")
    qs_d = nc.dram_tensor("qs", [2, HD], F32, kind="ExternalInput")
    ks_d = nc.dram_tensor("ks", [2, HD], F32, kind="ExternalInput")
    woi_d = nc.dram_tensor("woi", [2, INNER, C], F32, kind="ExternalInput")
    lg_d = nc.dram_tensor("lg", [2, C], F32, kind="ExternalInput")
    lb_d = nc.dram_tensor("lb", [2, C], F32, kind="ExternalInput")
    w1_d = nc.dram_tensor("w1p", [2, C, 2 * FPAD], F32, kind="ExternalInput")
    b1_d = nc.dram_tensor("b1p", [2, 2 * FPAD], F32, kind="ExternalInput")
    w2_d = nc.dram_tensor("w2p", [2, FPAD, C], F32, kind="ExternalInput")
    b2_d = nc.dram_tensor("b2", [2, C], F32, kind="ExternalInput")

    y1_s = nc.dram_tensor("y1s", [S, CC, 128, T], F32, kind=dk("y1s"))
    x2_s = nc.dram_tensor("x2s", [S, CC, 128, T], F32, kind=dk("x2s"))
    x3_s = nc.dram_tensor("x3s", [S, CC, 128, T], F32, kind=dk("x3s"))
    x4_s = nc.dram_tensor("x4s", [S, CC, 128, T], F32, kind=dk("x4s"))
    h2_s = nc.dram_tensor("h2s", [S, FC, 128, T], BF16, kind=dk("h2s"))
    nm_s = nc.dram_tensor("nms", [128 * 32], BF16, kind="Internal")
    dbg_s = nc.dram_tensor("dbgs", [40, 128, 512], F32, kind=dk("dbgs"))

    cd = {k: nc.inline_tensor(v, f"c_{k}") for k, v in HC.items()}

    big = ExitStack()
    P.alloc_sems(big)
    ident_f = big.enter_context(nc.sbuf_tensor("ident_f", [128, 128], F32))
    ones1x128f = big.enter_context(nc.sbuf_tensor("o1x128f", [1, 128], F32))
    ones1x128b = big.enter_context(nc.sbuf_tensor("o1x128b", [1, 128], BF16))
    ones1x64f = big.enter_context(nc.sbuf_tensor("o1x64f", [1, 64], F32))
    ones1x16f = big.enter_context(nc.sbuf_tensor("o1x16f", [1, 16], F32))
    ones128x1f = big.enter_context(nc.sbuf_tensor("o128x1f", [128, 1], F32))
    magic = big.enter_context(nc.sbuf_tensor("magic", [16, 512], I32))

    def ldc(name, dst, wkey=None):
        P.op("sync", lambda E, d=dst, sr=cd[name]: E.dma_start(out=d[:], in_=sr.ap()),
             writes=[wkey or (dst.name,)], dma=True)

    ldc("ident_f32", ident_f)
    ldc("ones_1x128_f32", ones1x128f)
    ldc("ones_1x128_bf16", ones1x128b)
    ldc("ones_1x64_f32", ones1x64f)
    ldc("ones_1x16_f32", ones1x16f)
    ldc("ones_128x1_f32", ones128x1f)
    ldc("magic_row", magic)

    DBG_ON = "dbgs" in taps

    def DBG(slot, ap, reads):
        if DBG_ON:
            a = _ap(ap)
            q = "gpsimd" if a.dtype != F32 else "sync"
            DMA(q, dbg_s.ap()[slot, 0:a.shape[0], 0:a.shape[1]], a, reads=reads)

    def MM(out, lhsT, rhs, reads, wkey, start, stop):
        out, lhsT, rhs = _ap(out), _ap(lhsT), _ap(rhs)
        P.op("tensor", lambda E, o=out, l_=lhsT, r=rhs, s_=start, st=stop:
             E.matmul(out=o, lhsT=l_, rhs=r, start=s_, stop=st), reads=reads, writes=[wkey])

    def TR(out, in_, reads, wkey):
        out, in_ = _ap(out), _ap(in_)
        pdim = in_.shape[0]
        P.op("tensor", lambda E, o=out, i=in_, p_=pdim: E.transpose(out=o, in_=i,
             identity=ident_f[0:p_, 0:p_]),
             reads=list(reads) + [(ident_f.name,)], writes=[wkey])

    def ACT(out, in_, func, reads, wkey, bias=0.0, scale=1.0):
        out, in_ = _ap(out), _ap(in_)
        P.op("scalar", lambda E, o=out, i=in_, f=func, b=bias, sc=scale:
             E.activation(out=o, in_=i, func=f, bias=b, scale=sc), reads=reads, writes=[wkey])

    def SCOPY(out, in_, reads, wkey):
        out, in_ = _ap(out), _ap(in_)
        P.op("scalar", lambda E, o=out, i=in_: E.copy(out=o, in_=i), reads=reads, writes=[wkey])

    def SADD(out, in_, add, reads, wkey):
        out, in_ = _ap(out), _ap(in_)
        P.op("scalar", lambda E, o=out, i=in_, a=add: E.add(out=o, in_=i, add=a),
             reads=reads, writes=[wkey])

    def VCOPY(out, in_, reads, wkey):
        out, in_ = _ap(out), _ap(in_)
        P.op("vector", lambda E, o=out, i=in_: E.tensor_copy(out=o, in_=i), reads=reads, writes=[wkey])

    def TT(out, in0, in1, op, reads, wkey):
        out, in0, in1 = _ap(out), _ap(in0), _ap(in1)
        P.op("vector", lambda E, o=out, a=in0, b=in1, p_=op: E.tensor_tensor(out=o, in0=a, in1=b, op=p_),
             reads=reads, writes=[wkey])

    def TS(out, in0, s1, s2, op0, op1, reads, wkey):
        out, in0 = _ap(out), _ap(in0)
        if s2 is None:
            P.op("vector", lambda E, o=out, a=in0, x=s1, p0=op0:
                 E.tensor_single_scalar(out=o, in_=a, scalar=x, op=p0),
                 reads=reads, writes=[wkey])
        else:
            P.op("vector", lambda E, o=out, a=in0, x=s1, y=s2, p0=op0, p1=op1:
                 E.tensor_scalar(out=o, in0=a, scalar1=x, scalar2=y, op0=p0, op1=p1),
                 reads=reads, writes=[wkey])

    def STT(out, in0, sc, in1, op0, op1, reads, wkey):
        out, in0, in1 = _ap(out), _ap(in0), _ap(in1)
        P.op("vector", lambda E, o=out, a=in0, s_=sc, b=in1, p0=op0, p1=op1:
             E.scalar_tensor_tensor(out=o, in0=a, scalar=s_, in1=b, op0=p0, op1=p1),
             reads=reads, writes=[wkey])

    def RED(out, in_, op, negate, reads, wkey):
        out, in_ = _ap(out), _ap(in_)
        P.op("vector", lambda E, o=out, i=in_, p_=op, n=negate:
             E.tensor_reduce(out=o, in_=i, axis=AX.X, op=p_, negate=n), reads=reads, writes=[wkey])

    def SCAN(out, d0, d1, init, reads, wkey):
        out, d0, d1, init = _ap(out), _ap(d0), _ap(d1), _ap(init)
        P.op("vector", lambda E, o=out, a=d0, b=d1, i=init:
             E.tensor_tensor_scan(out=o, data0=a, data1=b, initial=i, op0=ALU.mult, op1=ALU.add),
             reads=reads, writes=[wkey])

    def MEMSET(eng, ap, val, wkey, reads=()):
        ap = _ap(ap)
        P.op(eng, lambda E, a=ap, v=val: E.memset(a, v), reads=reads, writes=[wkey])

    def DMA(q, out, in_, reads=(), wkey=None, slow=False):
        writes = [wkey] if wkey else []
        P.op(q, lambda E, o=out, i=in_, sl=slow: E.dma_start(out=o, in_=i, allow_slow_non_contiguous=sl)
             if sl else E.dma_start(out=o, in_=i), reads=reads, writes=writes, dma=True)

    def newton_rsqrt(base_key, ve, y, t1, t2, m_ap):
        P.op("vector", lambda E, o=t1, i=ve: E.tensor_scalar(
            out=o.bitcast(I32), in0=i.bitcast(I32), scalar1=1, scalar2=None, op0=ALU.arith_shift_right),
            reads=[(base_key, "ve")], writes=[(base_key, "t1")])
        P.op("vector", lambda E, o=y, m=m_ap, i=t1: E.tensor_tensor(
            out=o.bitcast(I32), in0=m.bitcast(I32), in1=i.bitcast(I32), op=ALU.subtract),
            reads=[(base_key, "t1"), (magic.name,)], writes=[(base_key, "y")])
        for _ in range(2):
            TT(t1, y, y, ALU.mult, [(base_key, "y")], (base_key, "t1"))
            TT(t2, t1, ve, ALU.mult, [(base_key, "t1"), (base_key, "ve")], (base_key, "t2"))
            TS(t2, t2, -0.5, 1.5, ALU.mult, ALU.add, [(base_key, "t2")], (base_key, "t2"))
            TT(y, y, t2, ALU.mult, [(base_key, "y"), (base_key, "t2")], (base_key, "y"))

    def load_cols(ph, dram_ap, n128, name, psum):
        tmp = ph.enter_context(nc.sbuf_tensor(f"ld_{name}", [max(n128, 1), 128], F32))
        dst = ph.enter_context(nc.sbuf_tensor(f"col_{name}", [128, n128], F32))
        DMA("sync", tmp[:], dram_ap.rearrange("(a p) -> a p", p=128), wkey=(name, "tmp"))
        TR(psum[:, 0:n128], tmp[:], [(name, "tmp")], (psum.name,))
        SCOPY(dst[:], psum[:, 0:n128], [(psum.name,)], (dst.name,))
        return dst

    def phase_a():
        ph = ExitStack()
        sb_ = lambda nm, shp, dt=F32: ph.enter_context(nc.sbuf_tensor(nm, shp, dt))
        ps_ = lambda nm, shp: ph.enter_context(nc.psum_tensor(nm, shp, F32))
        wk16 = sb_("wk16", [128, 8 * 1024], BF16)
        wv16 = sb_("wv16", [128, 8 * 1024], BF16)
        wr16 = sb_("wr16", [128, 8 * 1024], BF16)
        wo16 = sb_("wo16", [128, 8 * 1024], BF16)
        x_tok = sb_("x_tok", [128, 2, 1024])
        xT = sb_("xT", [128, CC, TW + 1])
        xk16 = sb_("xk16", [128, CC, TW], BF16)
        xv16 = sb_("xv16", [128, CC, TW], BF16)
        xr16 = sb_("xr16", [128, CC, TW], BF16)
        dif = sb_("dif", [128, TW])
        ekf = [sb_("ekf0", [128, TW]), sb_("ekf1", [128, TW])]
        vf = [sb_("vf0", [128, TW]), sb_("vf1", [128, TW])]
        ekv = [sb_("ekv0", [128, TW]), sb_("ekv1", [128, TW])]
        rf = sb_("rf", [128, TW])
        t3f = sb_("t3f", [128, TW])
        sa = sb_("sa", [128, TW + 1])
        sac = sb_("sac", [128, CC])
        sbr = sb_("sbr", [16, TW + 1])
        sbc = sb_("sbc", [16, 1])
        den = sb_("den", [16, TW])
        rdr = sb_("rdr", [16, TW])
        numb = sb_("numb", [128, CC, TW])
        rwkv16 = sb_("rwkv16", [128, CC, TW], BF16)
        zbuf = sb_("zbuf", [128, CC, TW])
        zsq = sb_("zsq", [128, TW])
        selbeu = sb_("selbeu", [128, 8 * 48])
        selw = sb_("selw", [16, 8 * 128])
        ind16 = sb_("ind16", [16, 128])
        bconst = sb_("bconst", [16, 8])
        tcols = sb_("tcols", [16, 16])
        w_all = sb_("w_all", [128, 8])
        eu_fac = sb_("eu_fac", [128, 8])
        ew_col = sb_("ew_col", [16, 1])
        tmpA = sb_("tmpA", [16, 128])
        p0 = ps_("pa0", [128, 512])
        p1 = ps_("pa1", [128, 512])
        p2 = ps_("pa2", [128, 512])
        p3 = ps_("pa3", [128, 512])
        p4 = ps_("pa4", [48, 512])
        p5 = ps_("pa5", [128, 512])
        p6 = ps_("pa6", [128, 512])
        p7 = ps_("pa7", [128, 512])
        rows = sb_("rowsAbuf", [1, 7 * 512])

        for (w16, wd) in ((wk16, wkT), (wv16, wvT), (wr16, wrT), (wo16, woT)):
            for jc in range(8):
                DMA("gpsimd", w16[:, jc * 1024:(jc + 1) * 1024],
                    wd.ap()[jc * 128:(jc + 1) * 128, :], wkey=(w16.name, jc))
        ldc("selbeu", selbeu)
        ldc("selw", selw)
        ldc("ind16", ind16)
        ldc("bconst", bconst)

        mk = load_cols(ph, mixd["mixk"].ap(), 8, "mk", p0)
        mv = load_cols(ph, mixd["mixv"].ap(), 8, "mv", p0)
        mr = load_cols(ph, mixd["mixr"].ap(), 8, "mr", p0)

        DMA("sync", tcols[0:1, 0:16], td_d.ap().unsqueeze(0), wkey=("td", "row"))
        DMA("sync", tcols[1:2, 0:16], tf_d.ap().unsqueeze(0), wkey=("tf", "row"))
        TR(p0[0:16, 0:2], tcols[0:2, 0:16], [("td", "row"), ("tf", "row")], (p0.name,))
        ACT(ew_col[:], p0[0:16, 0:1], AF.Exp, [(p0.name,)], ("ew", "e1"))
        ACT(ew_col[:], ew_col[:], AF.Exp, [("ew", "e1")], (ew_col.name,), scale=-1.0)
        SCOPY(tcols[:, 2:3], p0[0:16, 1:2], [(p0.name,)], ("tf", "col"))
        TS(tmpA[:], ind16[:], ew_col[:, 0:1], None, ALU.mult, ALU.add,
           [(ew_col.name,), (ind16.name,)], (tmpA.name,))
        MM(p1[:, 0:8], tmpA[:], bconst[:], [(tmpA.name,), (bconst.name,)], (p1.name,), True, True)
        SCOPY(w_all[:], p1[:, 0:8], [(p1.name,)], (w_all.name,))
        TS(tmpA[:], ind16[:], tcols[:, 2:3], None, ALU.mult, ALU.add,
           [("tf", "col"), (ind16.name,)], (tmpA.name,))
        MM(p2[:, 0:8], tmpA[:], bconst[:], [(tmpA.name,), (bconst.name,)], (p2.name,), True, True)
        ACT(eu_fac[:], p2[:, 0:8], AF.Exp, [(p2.name,)], (eu_fac.name,))
        for ic in range(8):
            TS(selbeu[:, ic * 48 + 32:ic * 48 + 48], selbeu[:, ic * 48 + 32:ic * 48 + 48],
               eu_fac[:, ic:ic + 1], None, ALU.mult, ALU.add,
               [(eu_fac.name,), (selbeu.name,)], (selbeu.name,))

        inv_c = 1.0 / C
        mu_r, ms_r = rows[0:1, 0:512], rows[0:1, 512:1024]
        ve_r, y_r = rows[0:1, 1024:1536], rows[0:1, 1536:2048]
        n1_r, n2_r = rows[0:1, 2048:2560], rows[0:1, 2560:3072]
        mur_r = rows[0:1, 3072:3584]
        RK = "rowsA"

        for s in range(S):
            for tc in range(NTC):
                t0 = tc * TW
                if tc == 0:
                    MEMSET("vector", xT[:, :, 0:1], 0.0, (xT.name, "c"), reads=[(xT.name, "d")])
                else:
                    VCOPY(xT[:, :, 0:1], xT[:, :, TW:TW + 1], [(xT.name, "d")], (xT.name, "c"))
                for tt in range(4):
                    DMA("sync", x_tok[:, tt % 2, :], x_in.ap()[s, t0 + tt * 128:t0 + (tt + 1) * 128, :],
                        wkey=(x_tok.name, tt % 2))
                    for cc in range(CC):
                        TR(p0[:, 0:128], x_tok[:, tt % 2, cc * 128:(cc + 1) * 128],
                           [(x_tok.name, tt % 2)], (p0.name,))
                        SCOPY(xT[:, cc, 1 + tt * 128:1 + (tt + 1) * 128], p0[:, 0:128],
                              [(p0.name,)], (xT.name, "d"))
                for cc in range(CC):
                    TT(dif, xT[:, cc, 1:TW + 1], xT[:, cc, 0:TW], ALU.subtract,
                       [(xT.name, "d"), (xT.name, "c")], (dif.name,))
                    for (dst, m) in ((xk16, mk), (xv16, mv), (xr16, mr)):
                        STT(dst[:, cc, :], dif[:], m[:, cc:cc + 1], xT[:, cc, 0:TW], ALU.mult, ALU.add,
                            [(dif.name,), (m.name,), (xT.name, "c"), (xT.name, "d")], (dst.name, cc))

                for ic in range(8):
                    par = ic % 2
                    pk = p1 if par == 0 else p5
                    pv = p2 if par == 0 else p6
                    for jc in range(8):
                        MM(pk[:], wk16[:, jc * 1024 + ic * 128:jc * 1024 + (ic + 1) * 128], xk16[:, jc, :],
                           [(wk16.name, jc), (xk16.name, jc)], (pk.name,), jc == 0, jc == 7)
                    for jc in range(8):
                        MM(pv[:], wv16[:, jc * 1024 + ic * 128:jc * 1024 + (ic + 1) * 128], xv16[:, jc, :],
                           [(wv16.name, jc), (xv16.name, jc)], (pv.name,), jc == 0, jc == 7)
                    ACT(ekf[par][:], pk[:], AF.Exp, [(pk.name,)], (ekf[par].name,))
                    SCOPY(vf[par][:], pv[:], [(pv.name,)], (vf[par].name,))
                    MM(p4[:], selbeu[:, ic * 48:(ic + 1) * 48], ekf[par][:],
                       [(selbeu.name,), (ekf[par].name,)], (p4.name,), ic == 0, ic == 7)
                    TT(ekv[par], ekf[par][:], vf[par][:], ALU.mult,
                       [(ekf[par].name,), (vf[par].name,)], (ekv[par].name,))
                    if tc == 0:
                        MEMSET("vector", sac[:, ic:ic + 1], 0.0, (sac.name, ic))
                    VCOPY(sa[:, 0:1], sac[:, ic:ic + 1], [(sac.name, ic)], (sa.name, "c"))
                    SCAN(sa[:, 1:TW + 1], w_all[:, ic:ic + 1].broadcast_to([128, TW]), ekv[par][:],
                         sac[:, ic:ic + 1], [(w_all.name,), (ekv[par].name,), (sac.name, ic)], (sa.name, "s"))
                    VCOPY(sac[:, ic:ic + 1], sa[:, TW:TW + 1], [(sa.name, "s")], (sac.name, ic))
                    TS(ekv[par], ekv[par][:], eu_fac[:, ic:ic + 1], None, ALU.mult, ALU.add,
                       [(ekv[par].name,), (eu_fac.name,)], (ekv[par].name,))
                    TT(numb[:, ic, :], sa[:, 0:TW], ekv[par][:], ALU.add,
                       [(sa.name, "c"), (sa.name, "s"), (ekv[par].name,)], (numb.name, ic))
                    if s == 0 and tc == 0 and ic < 2:
                        DBG(2 + ic * 4, ekf[par], [(ekf[par].name,)])
                        DBG(3 + ic * 4, vf[par], [(vf[par].name,)])
                        DBG(4 + ic * 4, sa[:, 0:512], [(sa.name, "s"), (sa.name, "c")])
                        DBG(5 + ic * 4, numb[:, ic, :], [(numb.name, ic)])
                    if s == 0 and tc == 1 and ic == 0:
                        DBG(21, ekf[par], [(ekf[par].name,)])
                        DBG(22, sa[:, 0:512], [(sa.name, "s"), (sa.name, "c")])
                        DBG(23, numb[:, ic, :], [(numb.name, ic)])

                if s == 0 and tc == 1:
                    DBG(19, xT[:, 0, 0:512], [(xT.name, "d"), (xT.name, "c")])
                    DBG(20, xk16[:, 0, :], [(xk16.name, 0)])
                if s == 0 and tc == 0:
                    DBG(0, xT[:, 0, 0:512], [(xT.name, "d"), (xT.name, "c")])
                    DBG(1, xT[:, 7, 0:512], [(xT.name, "d"), (xT.name, "c")])
                    DBG(16, x_tok[:, 1, 0:512], [(x_tok.name, 1)])
                    DBG(17, ident_f[:, 0:128], [(ident_f.name,)])
                    DBG(18, xk16[:, 0, :], [(xk16.name, 0)])
                if tc == 0:
                    MEMSET("vector", sbc[:], 0.0, (sbc.name,))
                VCOPY(sbr[:, 0:1], sbc[:], [(sbc.name,)], (sbr.name, "c"))
                SCAN(sbr[:, 1:TW + 1], ew_col[:].broadcast_to([16, TW]), p4[0:16, :], sbc[:],
                     [(ew_col.name,), (p4.name,), (sbc.name,)], (sbr.name, "s"))
                VCOPY(sbc[:], sbr[:, TW:TW + 1], [(sbr.name, "s")], (sbc.name,))
                TS(den, p4[32:48, :], 1e-6, None, ALU.max, ALU.add, [(p4.name,)], (den.name,))
                TT(den, den[:], sbr[:, 0:TW], ALU.add,
                   [(den.name,), (sbr.name, "c"), (sbr.name, "s")], (den.name,))
                P.op("vector", lambda E: E.reciprocal(out=rdr[:], in_=den[:]),
                     reads=[(den.name,)], writes=[(rdr.name,)])
                if s == 0 and tc == 1:
                    DBG(24, den, [(den.name,)])
                    DBG(25, sbr[0:16, 0:512], [(sbr.name, "s"), (sbr.name, "c")])
                if s == 0 and tc == 0:
                    DBG(10, den, [(den.name,)])
                    DBG(11, rdr, [(rdr.name,)])
                    DBG(12, sbr[0:16, 0:512], [(sbr.name, "s"), (sbr.name, "c")])

                for ic in range(8):
                    for jc in range(8):
                        MM(p3[:], wr16[:, jc * 1024 + ic * 128:jc * 1024 + (ic + 1) * 128], xr16[:, jc, :],
                           [(wr16.name, jc), (xr16.name, jc)], (p3.name,), jc == 0, jc == 7)
                    ACT(rf[:], p3[:], AF.Tanh, [(p3.name,)], (rf.name,), scale=0.5)
                    TS(rf, rf[:], 0.5, 0.5, ALU.mult, ALU.add, [(rf.name,)], (rf.name,))
                    MM(p3[:], selw[:, ic * 128:(ic + 1) * 128], rdr[:],
                       [(selw.name,), (rdr.name,)], (p3.name,), True, True)
                    TT(t3f, numb[:, ic, :], p3[:], ALU.mult,
                       [(numb.name, ic), (p3.name,)], (t3f.name,))
                    TT(rwkv16[:, ic, :], t3f[:], rf[:], ALU.mult,
                       [(t3f.name,), (rf.name,)], (rwkv16.name, ic))
                    if s == 0 and tc == 0 and ic == 0:
                        DBG(13, t3f, [(t3f.name,)])
                        DBG(14, rf, [(rf.name,)])

                for cc in range(CC):
                    for ic in range(8):
                        MM(p7[:], wo16[:, ic * 1024 + cc * 128:ic * 1024 + (cc + 1) * 128], rwkv16[:, ic, :],
                           [(wo16.name, ic), (rwkv16.name, ic)], (p7.name,), ic == 0, ic == 7)
                    TT(zbuf[:, cc, :], p7[:], xT[:, cc, 1:TW + 1], ALU.add,
                       [(p7.name,), (xT.name, "d")], (zbuf.name, cc))
                    P.op("scalar", lambda E, c_=cc: E.square(out=zsq[:], in_=zbuf[:, c_, :]),
                         reads=[(zbuf.name, cc)], writes=[(zsq.name,)])
                    MM(p1[0:1, :], ones128x1f[:], zbuf[:, cc, :],
                       [(zbuf.name, cc), (ones128x1f.name,)], (p1.name,), cc == 0, cc == 7)
                    MM(p5[0:1, :], ones128x1f[:], zsq[:],
                       [(zsq.name,), (ones128x1f.name,)], (p5.name,), cc == 0, cc == 7)
                if s == 0 and tc == 0:
                    DBG(15, zbuf[:, 0, :], [(zbuf.name, 0)])
                TS(mu_r, p1[0:1, :], inv_c, None, ALU.mult, ALU.add, [(p1.name,)], (RK, "mu"))
                TS(ms_r, p5[0:1, :], inv_c, None, ALU.mult, ALU.add, [(p5.name,)], (RK, "ms"))
                TT(ve_r, mu_r, mu_r, ALU.mult, [(RK, "mu")], (RK, "ve0"))
                TT(ve_r, ms_r, ve_r, ALU.subtract, [(RK, "ms"), (RK, "ve0")], (RK, "ve1"))
                TS(ve_r, ve_r, 1e-5, None, ALU.add, ALU.add, [(RK, "ve1")], (RK, "ve"))
                newton_rsqrt(RK, ve_r, y_r, n1_r, n2_r, magic[0:1, :])
                TT(mur_r, mu_r, y_r, ALU.mult, [(RK, "mu"), (RK, "y")], (RK, "mur"))
                MM(p2[:], ones1x128f[:], y_r, [(RK, "y"), (ones1x128f.name,)], (p2.name,), True, True)
                MM(p6[:], ones1x128f[:], mur_r, [(RK, "mur"), (ones1x128f.name,)], (p6.name,), True, True)
                for cc in range(CC):
                    TT(zbuf[:, cc, :], zbuf[:, cc, :], p2[:], ALU.mult,
                       [(zbuf.name, cc), (p2.name,)], (zbuf.name, cc))
                    TT(zbuf[:, cc, :], zbuf[:, cc, :], p6[:], ALU.subtract,
                       [(zbuf.name, cc), (p6.name,)], (zbuf.name, cc))
                    DMA("sync", y1_s.ap()[s, cc, :, t0:t0 + TW], zbuf[:, cc, :], reads=[(zbuf.name, cc)])
        P.barrier()
        P.emit_phase()
        ph.close()

    def phase_b(l, src_s, dst_s):
        ph = ExitStack()
        sb_ = lambda nm, shp, dt=F32: ph.enter_context(nc.sbuf_tensor(f"{nm}_l{l}", shp, dt))
        wq16 = sb_("wq16", [128, 8 * 1024], BF16)
        wv16i = sb_("wv16i", [128, 8 * 512], BF16)
        woi16 = sb_("woi16", [128, 4 * 1024], BF16)
        tabs = sb_("tabs", [128, 4, TW])
        ybuf = sb_("ybuf", [128, CC, TW + 1])
        h16 = sb_("h16", [128, CC, TW], BF16)
        qkf = sb_("qkf", [128, 8, TW])
        sqq = sb_("sqq", [128, TW])
        shf = sb_("shf", [128, TW])
        tt1 = sb_("tt1", [128, TW])
        tt2 = sb_("tt2", [128, TW])
        qbuf = sb_("qbuf", [64, AH, TW], BF16)
        kbuf = sb_("kbuf", [64, AH, 1024], BF16)
        vbuf = sb_("vbuf", [128, 16, AH, 65], BF16)
        expT = sb_("expT", [128, 8 * TW], BF16)
        negm = sb_("negm", [128, 33])
        negm16 = sb_("negm16", [128, 32], BF16)
        nrow = sb_("nrow", [1, 32 * 128], BF16)
        rz = sb_("rz", [1, TW])
        rzs = sb_("rzs", [64, TW])
        obuf = sb_("obuf", [128, 4 * TW], BF16)
        x2b = sb_("x2b", [128, 2, TW])
        selssq = sb_("selssq", [128, 8 * 16])
        selrn = sb_("selrn", [16, 8 * 128])
        sccol = sb_("sccol", [128, 2])
        mlow = sb_("mlow", [128, 128])
        mhigh = sb_("mhigh", [128, 128])
        tge = sb_("tge", [128, 128], BF16)
        tle = sb_("tle", [128, 128], BF16)
        ps_ = lambda nm, shp: ph.enter_context(nc.psum_tensor(f"{nm}_l{l}", shp, F32))
        p0 = ps_("pb0", [128, 512])
        p1 = ps_("pb1", [128, 512])
        p2 = ps_("pb2", [128, 512])
        p3 = ps_("pb3", [128, 512])
        p4 = ps_("pb4", [128, 128])
        p5 = ps_("pb5", [128, 512])
        p6 = ps_("pb6", [128, 512])
        p7 = ps_("pb7", [65, 512])
        rows = sb_("rowsBbuf", [16, 4 * 512])

        for jc in range(8):
            DMA("gpsimd", wq16[:, jc * 1024:(jc + 1) * 1024],
                wqkv_d.ap()[l, jc * 128:(jc + 1) * 128, 0:1024], wkey=(wq16.name, jc))
            DMA("gpsimd", wv16i[:, jc * 512:(jc + 1) * 512],
                wqkv_d.ap()[l, jc * 128:(jc + 1) * 128, 1024:1536], wkey=(wv16i.name, jc))
        for icc in range(4):
            DMA("gpsimd", woi16[:, icc * 1024:(icc + 1) * 1024],
                woi_d.ap()[l, icc * 128:(icc + 1) * 128, :], wkey=(woi16.name, icc))
        ldc("sel_ssq", selssq)
        ldc("sel_rn", selrn)
        ldc("mask_low", mlow)
        ldc("mask_high", mhigh)
        ldc("t_ge", tge)
        ldc("t_le", tle)
        MEMSET("gpsimd", vbuf[:], 0.0, (vbuf.name, "init"))
        P.op("gpsimd", lambda E: E.memset(vbuf[:, :, :, 64:65], 1.0),
             reads=[(vbuf.name, "init")], writes=[(vbuf.name, "ones")])
        DMA("sync", sccol[0:64, 0:1], qs_d.ap()[l].unsqueeze(1), wkey=(sccol.name, "q0"))
        DMA("sync", sccol[0:64, 1:2], ks_d.ap()[l].unsqueeze(1), wkey=(sccol.name, "k0"))
        VCOPY(sccol[64:128, 0:2], sccol[0:64, 0:2], [(sccol.name, "q0"), (sccol.name, "k0")], (sccol.name,))

        ve_a, y_a = rows[0:16, 0:512], rows[0:16, 512:1024]
        n1_a, n2_a = rows[0:16, 1024:1536], rows[0:16, 1536:2048]
        RK = "rowsB"
        SCL = {0: (sccol[:, 0:1], (sccol.name,)), 1: (sccol[:, 0:1], (sccol.name,)),
               2: (sccol[:, 1:2], (sccol.name,)), 3: (sccol[:, 1:2], (sccol.name,))}

        for s in range(S):
            for tc in range(NTC):
                t0 = tc * TW
                w = tc
                if tc == 0:
                    MEMSET("vector", ybuf[:, :, 0:1], 0.0, (ybuf.name, "c"),
                           reads=[(ybuf.name, cc) for cc in range(CC)])
                else:
                    P.op("vector", lambda E: E.tensor_copy(out=ybuf[:, :, 0:1], in_=ybuf[:, :, TW:TW + 1]),
                         reads=[(ybuf.name, cc) for cc in range(CC)], writes=[(ybuf.name, "c")])
                for cc in range(CC):
                    DMA("sync", ybuf[:, cc, 1:TW + 1], src_s.ap()[s, cc, :, t0:t0 + TW],
                        wkey=(ybuf.name, cc))
                for cc in range(CC):
                    if cc < 4:
                        VCOPY(h16[:, cc, :], ybuf[:, cc, 1:TW + 1], [(ybuf.name, cc)], (h16.name, cc))
                    else:
                        VCOPY(h16[:, cc, :], ybuf[:, cc, 0:TW],
                              [(ybuf.name, cc), (ybuf.name, "c")], (h16.name, cc))
                if tc > 0:
                    VCOPY(kbuf[:, :, 0:512], kbuf[:, :, 512:1024],
                          [(kbuf.name, "cur")], (kbuf.name, "prev"))

                for i4 in range(4):
                    nmt = ("tab_qc", "tab_qs", "tab_kc", "tab_ks")[i4]
                    DMA("sync", tabs[:, i4, :], cd[nmt].ap()[:, t0:t0 + TW], wkey=(tabs.name, (i4, "raw")))
                    sc_ap, sc_key = SCL[i4]
                    TS(tabs[:, i4, :], tabs[:, i4, :], sc_ap, None, ALU.mult, ALU.add,
                       [(tabs.name, (i4, "raw")), sc_key], (tabs.name, i4))
                for i8 in range(8):
                    for jc in range(8):
                        MM(p0[:], wq16[:, jc * 1024 + i8 * 128:jc * 1024 + (i8 + 1) * 128], h16[:, jc, :],
                           [(wq16.name, jc), (h16.name, jc)], (p0.name,), jc == 0, jc == 7)
                    SCOPY(qkf[:, i8, :], p0[:], [(p0.name,)], (qkf.name, i8))
                    P.op("scalar", lambda E, i=i8: E.square(out=sqq[:], in_=p0[:]),
                         reads=[(p0.name,)], writes=[(sqq.name,)])
                    MM(p1[0:16, :], selssq[:, i8 * 16:(i8 + 1) * 16], sqq[:],
                       [(sqq.name,), (selssq.name,)], (p1.name,), i8 == 0, i8 == 7)
                for tt in range(4):
                    for jc in range(8):
                        MM(p2[:], h16[:, jc, tt * 128:(tt + 1) * 128], wv16i[:, jc * 512:(jc + 1) * 512],
                           [(wv16i.name, jc), (h16.name, jc)], (p2.name,), jc == 0, jc == 7)
                    slot = tc * 4 + tt
                    P.op("scalar", lambda E, sl=slot: E.copy(
                        out=vbuf[:, sl, :, 0:64], in_=p2[:].rearrange("p (h d) -> p h d", h=8)),
                        reads=[(p2.name,), (vbuf.name, "ones")], writes=[(vbuf.name, slot)])

                VCOPY(ve_a, p1[0:16, :], [(p1.name,)], (RK, "ve"))
                newton_rsqrt(RK, ve_a, y_a, n1_a, n2_a, magic[:, :])
                TS(y_a, y_a, 1e12, None, ALU.min, ALU.add, [(RK, "y")], (RK, "y"))

                for i8 in range(8):
                    isq = i8 < 4
                    oc = i8 % 4
                    cb, sbt = (0, 1) if isq else (2, 3)
                    MM(p2[:], selrn[:, i8 * 128:(i8 + 1) * 128], y_a,
                       [(RK, "y"), (selrn.name,)], (p2.name,), True, True)
                    TT(tt1, qkf[:, i8, :], tabs[:, cb, :], ALU.mult,
                       [(qkf.name, i8), (tabs.name, cb)], (tt1.name,))
                    for (a0, b0) in ((0, 32), (32, 0), (64, 96), (96, 64)):
                        TT(tt2[a0:a0 + 32, :], qkf[b0:b0 + 32, i8, :], tabs[b0:b0 + 32, sbt, :],
                           ALU.mult, [(qkf.name, i8), (tabs.name, sbt)], (tt2.name,))
                    TT(tt1, tt1[:], tt2[:], ALU.add, [(tt1.name,), (tt2.name,)], (tt1.name,))
                    for half in range(2):
                        h = 2 * oc + half
                        if isq:
                            dst_ap = qbuf[:, h, :]
                            wk = (qbuf.name, h)
                        else:
                            dst_ap = kbuf[:, h, 512:1024]
                            wk = (kbuf.name, "cur")
                        TT(dst_ap, tt1[half * 64:half * 64 + 64, :], p2[half * 64:half * 64 + 64, :],
                           ALU.mult, [(tt1.name,), (p2.name,)], wk)

                for h in range(AH):
                    for a in range(4):
                        qsl = qbuf[:, h, a * 128:(a + 1) * 128]
                        if w == 0:
                            ncols = (a + 1) * 128
                            MM(p3[:, 0:ncols], qsl, kbuf[:, h, 512:512 + ncols],
                               [(qbuf.name, h), (kbuf.name, "cur")], (p3.name,), True, True)
                            TT(p3[:, ncols - 128:ncols], p3[:, ncols - 128:ncols], mhigh[:], ALU.add,
                               [(p3.name,), (mhigh.name,)], (p3.name,))
                            RED(negm[:, h * 4 + a:h * 4 + a + 1], p3[:, 0:ncols], ALU.max, True,
                                [(p3.name,)], (negm.name, (h, a)))
                        else:
                            j0 = a * 128
                            MM(p3[:], qsl, kbuf[:, h, j0:j0 + 512],
                               [(qbuf.name, h), (kbuf.name, "prev"), (kbuf.name, "cur")],
                               (p3.name,), True, True)
                            MM(p4[:], qsl, kbuf[:, h, j0 + 512:j0 + 640],
                               [(qbuf.name, h), (kbuf.name, "cur")], (p4.name,), True, True)
                            TT(p3[:, 0:128], p3[:, 0:128], mlow[:], ALU.add,
                               [(p3.name,), (mlow.name,)], (p3.name,))
                            TT(p4[:], p4[:], mhigh[:], ALU.add, [(p4.name,), (mhigh.name,)], (p4.name,))
                            RED(negm[:, h * 4 + a:h * 4 + a + 1], p3[:], ALU.max, True,
                                [(p3.name,)], (negm.name, (h, a)))
                            RED(negm[:, 32:33], p4[:], ALU.max, True, [(p4.name,)], (negm.name, "x"))
                            P.op("vector", lambda E, h_=h, a_=a: E.tensor_tensor(
                                out=negm[:, h_ * 4 + a_:h_ * 4 + a_ + 1],
                                in0=negm[:, h_ * 4 + a_:h_ * 4 + a_ + 1], in1=negm[:, 32:33], op=ALU.min),
                                reads=[(negm.name, (h, a)), (negm.name, "x")], writes=[(negm.name, (h, a))])
                VCOPY(negm16[:], negm[:, 0:32], [(negm.name, (h, a)) for h in range(AH) for a in range(4)],
                      (negm16.name,))
                DMA("gpsimd", nm_s.ap().rearrange("(p q) -> p q", q=32), negm16[:],
                    reads=[(negm16.name,)], wkey=("nms",))
                DMA("gpsimd", nrow[0:1, :].rearrange("o (q p) -> o q p", q=32), nm_s.ap().rearrange("(p q) -> q p", q=32).unsqueeze(0),
                    reads=[("nms",)], wkey=(nrow.name,), slow=True)

                for h in range(AH):
                    MEMSET("gpsimd", expT[:], 0.0, (expT.name,))
                    blist = range(4, 8) if w == 0 else range(8)
                    for b in blist:
                        a0 = max(0, b - 4)
                        iend = 512 if b >= 4 else (b + 1) * 128
                        istart = a0 * 128
                        iw = iend - istart
                        j0 = b * 128
                        kk = (kbuf.name, "prev") if b < 4 else (kbuf.name, "cur")
                        MM(p6[:, 0:iw], kbuf[:, h, j0:j0 + 128], qbuf[:, h, istart:iend],
                           [(qbuf.name, h), kk], (p6.name,), True, False)
                        MM(p6[:, 0:iw], ones1x128b[:], nrow[0:1, h * 512 + istart:h * 512 + iend],
                           [(ones1x128b.name,), (nrow.name,)], (p6.name,), False, True)
                        ACT(expT[:, b * 512 + istart:b * 512 + iend], p6[:, 0:iw], AF.Exp,
                            [(p6.name,)], (expT.name,))
                        if b >= 4:
                            tri0 = b * 512 + a0 * 128
                            TT(expT[:, tri0:tri0 + 128], expT[:, tri0:tri0 + 128], tge[:], ALU.mult,
                               [(expT.name,), (tge.name,)], (expT.name,))
                        else:
                            tri0 = b * 512 + b * 128
                            TT(expT[:, tri0:tri0 + 128], expT[:, tri0:tri0 + 128], tle[:], ALU.mult,
                               [(expT.name,), (tle.name,)], (expT.name,))
                    first = True
                    for b in blist:
                        slot = (w - 1) * 4 + b
                        MM(p7[:], vbuf[:, slot, h, :], expT[:, b * 512:(b + 1) * 512],
                           [(vbuf.name, slot), (vbuf.name, "ones"), (expT.name,)], (p7.name,),
                           first, b == max(blist))
                        first = False
                    P.op("vector", lambda E: E.reciprocal(out=rz[:], in_=p7[64:65, :]),
                         reads=[(p7.name,)], writes=[(rz.name,)])
                    MM(p2[0:64, :], ones1x64f[:], rz[:], [(rz.name,), (ones1x64f.name,)],
                       (p2.name,), True, True)
                    SCOPY(rzs[:], p2[0:64, :], [(p2.name,)], (rzs.name,))
                    r0 = (h % 2) * 64
                    c0 = (h // 2) * 512
                    TT(obuf[r0:r0 + 64, c0:c0 + 512], p7[0:64, :], rzs[:], ALU.mult,
                       [(p7.name,), (rzs.name,)], (obuf.name, h))

                for cc in range(CC):
                    for icc in range(4):
                        MM(p0[:], woi16[:, icc * 1024 + cc * 128:icc * 1024 + (cc + 1) * 128],
                           obuf[:, icc * 512:(icc + 1) * 512],
                           [(woi16.name, icc)] + [(obuf.name, 2 * icc), (obuf.name, 2 * icc + 1)],
                           (p0.name,), icc == 0, icc == 3)
                    TT(x2b[:, cc % 2, :], p0[:], ybuf[:, cc, 1:TW + 1], ALU.add,
                       [(p0.name,), (ybuf.name, cc)], (x2b.name, cc % 2))
                    DMA("sync", dst_s.ap()[s, cc, :, t0:t0 + TW], x2b[:, cc % 2, :],
                        reads=[(x2b.name, cc % 2)])
        P.barrier()
        P.emit_phase()
        ph.close()

    def phase_c1(l, src_s):
        ph = ExitStack()
        sb_ = lambda nm, shp, dt=F32: ph.enter_context(nc.sbuf_tensor(f"{nm}_c{l}", shp, dt))
        w1sb = sb_("w1sb", [128, 8 * 2 * FPAD], BF16)
        xbuf = sb_("xbufc", [128, CC, TW + 1])
        ln16 = sb_("ln16", [128, CC, TW], BF16)
        hsq = sb_("hsq", [128, TW])
        af = sb_("af", [128, TW])
        gf = sb_("gf", [128, TW])
        h216 = sb_("h216", [128, 2, TW], BF16)
        ps_ = lambda nm, shp: ph.enter_context(nc.psum_tensor(f"{nm}_c{l}", shp, F32))
        p0 = ps_("pc0", [128, 512])
        p1 = ps_("pc1", [128, 512])
        p2 = ps_("pc2", [128, 512])
        p3 = ps_("pc3", [128, 512])
        p4 = ps_("pc4", [128, 512])
        p5 = ps_("pc5", [128, 512])
        p6 = ps_("pc6", [128, 512])
        p7 = ps_("pc7", [128, 512])
        rows = sb_("rowsCbuf", [1, 6 * 512])

        for jc in range(8):
            DMA("gpsimd", w1sb[:, jc * 2 * FPAD:(jc + 1) * 2 * FPAD],
                w1_d.ap()[l, jc * 128:(jc + 1) * 128, :], wkey=(w1sb.name, jc))
        lg = load_cols(ph, lg_d.ap()[l], 8, f"lg{l}", p6)
        lb = load_cols(ph, lb_d.ap()[l], 8, f"lb{l}", p6)
        b1c = load_cols(ph, b1_d.ap()[l], 2 * FC, f"b1c{l}", p6)

        inv_c = 1.0 / C
        mu_r, ms_r = rows[0:1, 0:512], rows[0:1, 512:1024]
        ve_r, y_r = rows[0:1, 1024:1536], rows[0:1, 1536:2048]
        n1_r, n2_r = rows[0:1, 2048:2560], rows[0:1, 2560:3072]
        RK = "rowsC"

        for s in range(S):
            for tc in range(NTC):
                t0 = tc * TW
                if tc == 0:
                    MEMSET("vector", xbuf[:, :, 0:1], 0.0, (xbuf.name, "c"),
                           reads=[(xbuf.name, cc) for cc in range(CC)])
                else:
                    P.op("vector", lambda E: E.tensor_copy(out=xbuf[:, :, 0:1], in_=xbuf[:, :, TW:TW + 1]),
                         reads=[(xbuf.name, cc) for cc in range(CC)], writes=[(xbuf.name, "c")])
                for cc in range(CC):
                    DMA("sync", xbuf[:, cc, 1:TW + 1], src_s.ap()[s, cc, :, t0:t0 + TW],
                        wkey=(xbuf.name, cc))
                hv = lambda cc: xbuf[:, cc, 1:TW + 1] if cc < 4 else xbuf[:, cc, 0:TW]
                hvk = lambda cc: [(xbuf.name, cc)] + ([] if cc < 4 else [(xbuf.name, "c")])
                for cc in range(CC):
                    P.op("scalar", lambda E, c_=cc, a=hv(cc): E.square(out=hsq[:], in_=a),
                         reads=hvk(cc), writes=[(hsq.name,)])
                    MM(p2[0:1, :], ones128x1f[:], hv(cc), hvk(cc) + [(ones128x1f.name,)],
                       (p2.name,), cc == 0, cc == 7)
                    MM(p3[0:1, :], ones128x1f[:], hsq[:], [(hsq.name,), (ones128x1f.name,)],
                       (p3.name,), cc == 0, cc == 7)
                TS(mu_r, p2[0:1, :], inv_c, None, ALU.mult, ALU.add, [(p2.name,)], (RK, "mu"))
                TS(ms_r, p3[0:1, :], inv_c, None, ALU.mult, ALU.add, [(p3.name,)], (RK, "ms"))
                TT(ve_r, mu_r, mu_r, ALU.mult, [(RK, "mu")], (RK, "ve0"))
                TT(ve_r, ms_r, ve_r, ALU.subtract, [(RK, "ms"), (RK, "ve0")], (RK, "ve1"))
                TS(ve_r, ve_r, 1e-5, None, ALU.add, ALU.add, [(RK, "ve1")], (RK, "ve"))
                newton_rsqrt(RK, ve_r, y_r, n1_r, n2_r, magic[0:1, :])
                MM(p4[:], ones1x128f[:], y_r, [(RK, "y"), (ones1x128f.name,)], (p4.name,), True, True)
                MM(p5[:], ones1x128f[:], mu_r, [(RK, "mu"), (ones1x128f.name,)], (p5.name,), True, True)
                for cc in range(CC):
                    TT(af, hv(cc), p5[:], ALU.subtract, hvk(cc) + [(p5.name,)], (af.name,))
                    TT(af, af[:], p4[:], ALU.mult, [(af.name,), (p4.name,)], (af.name,))
                    TS(ln16[:, cc, :], af[:], lg[:, cc:cc + 1], lb[:, cc:cc + 1], ALU.mult, ALU.add,
                       [(af.name,), (lg.name,), (lb.name,)], (ln16.name, cc))
                for fc2 in range(FC):
                    pa = p0 if fc2 % 2 == 0 else p6
                    pg = p1 if fc2 % 2 == 0 else p7
                    for jc in range(8):
                        MM(pa[:], w1sb[:, jc * 2 * FPAD + fc2 * 128:jc * 2 * FPAD + (fc2 + 1) * 128],
                           ln16[:, jc, :], [(w1sb.name, jc), (ln16.name, jc)], (pa.name,), jc == 0, jc == 7)
                    for jc in range(8):
                        MM(pg[:], w1sb[:, jc * 2 * FPAD + FPAD + fc2 * 128:jc * 2 * FPAD + FPAD + (fc2 + 1) * 128],
                           ln16[:, jc, :], [(w1sb.name, jc), (ln16.name, jc)], (pg.name,), jc == 0, jc == 7)
                    SADD(af[:], pa[:], b1c[:, fc2:fc2 + 1], [(pa.name,), (b1c.name,)], (af.name,))
                    ACT(gf[:], pg[:], AF.Gelu, [(pg.name,), (b1c.name,)], (gf.name,),
                        bias=b1c[:, FC + fc2:FC + fc2 + 1])
                    TT(h216[:, fc2 % 2, :], af[:], gf[:], ALU.mult,
                       [(af.name,), (gf.name,)], (h216.name, fc2 % 2))
                    DMA("sync", h2_s.ap()[s, fc2, :, t0:t0 + TW], h216[:, fc2 % 2, :],
                        reads=[(h216.name, fc2 % 2)])
        P.barrier()
        P.emit_phase()
        ph.close()

    def phase_c2(l, src_s, dst_s, final):
        ph = ExitStack()
        sb_ = lambda nm, shp, dt=F32: ph.enter_context(nc.sbuf_tensor(f"{nm}_d{l}", shp, dt))
        w2sb = sb_("w2sb", [128, FC * 1024], BF16)
        h2b = sb_("h2b", [128, FC, TW], BF16)
        xbuf = sb_("xbuf2", [128, CC, TW])
        ybuf2 = sb_("ybuf2", [128, CC, TW]) if final else None
        zb = sb_("zb", [128, CC, TW])
        zsq2 = sb_("zsq2", [128, TW])
        otok = sb_("otok", [128, 2, 1024]) if final else None
        ps_ = lambda nm, shp: ph.enter_context(nc.psum_tensor(f"{nm}_d{l}", shp, F32))
        p0 = ps_("pd0", [128, 512])
        p1 = ps_("pd1", [128, 512])
        p2 = ps_("pd2", [128, 512])
        p3 = ps_("pd3", [128, 512])
        p4 = ps_("pd4", [128, 512])
        p5 = ps_("pd5", [128, 512])
        p6 = ps_("pd6", [128, 512])
        rows = sb_("rowsDbuf", [1, 7 * 512])

        for fc in range(FC):
            DMA("gpsimd", w2sb[:, fc * 1024:(fc + 1) * 1024],
                w2_d.ap()[l, fc * 128:(fc + 1) * 128, :], wkey=(w2sb.name, fc))
        b2c = load_cols(ph, b2_d.ap()[l], 8, f"b2c{l}", p6)

        inv_c = 1.0 / C
        mu_r, ms_r = rows[0:1, 0:512], rows[0:1, 512:1024]
        ve_r, y_r = rows[0:1, 1024:1536], rows[0:1, 1536:2048]
        n1_r, n2_r = rows[0:1, 2048:2560], rows[0:1, 2560:3072]
        mur_r = rows[0:1, 3072:3584]
        RK = "rowsD"

        for s in range(S):
            for tc in range(NTC):
                t0 = tc * TW
                for fc in range(FC):
                    DMA("sync", h2b[:, fc, :], h2_s.ap()[s, fc, :, t0:t0 + TW], wkey=(h2b.name, fc))
                for cc in range(CC):
                    DMA("sync", xbuf[:, cc, :], src_s.ap()[s, cc, :, t0:t0 + TW], wkey=(xbuf.name, cc))
                    if final:
                        DMA("sync", ybuf2[:, cc, :], y1_s.ap()[s, cc, :, t0:t0 + TW], wkey=(ybuf2.name, cc))
                for cc in range(CC):
                    pw = p0 if cc % 2 == 0 else p1
                    for fc in range(FC):
                        MM(pw[:], w2sb[:, fc * 1024 + cc * 128:fc * 1024 + (cc + 1) * 128], h2b[:, fc, :],
                           [(w2sb.name, fc), (h2b.name, fc)], (pw.name,), fc == 0, fc == FC - 1)
                    STT(zb[:, cc, :], pw[:], b2c[:, cc:cc + 1], xbuf[:, cc, :], ALU.add, ALU.add,
                        [(pw.name,), (b2c.name,), (xbuf.name, cc)], (zb.name, cc))
                    if final:
                        TT(zb[:, cc, :], zb[:, cc, :], ybuf2[:, cc, :], ALU.add,
                           [(zb.name, cc), (ybuf2.name, cc)], (zb.name, cc))
                        P.op("scalar", lambda E, c_=cc: E.square(out=zsq2[:], in_=zb[:, c_, :]),
                             reads=[(zb.name, cc)], writes=[(zsq2.name,)])
                        MM(p2[0:1, :], ones128x1f[:], zb[:, cc, :],
                           [(zb.name, cc), (ones128x1f.name,)], (p2.name,), cc == 0, cc == 7)
                        MM(p3[0:1, :], ones128x1f[:], zsq2[:], [(zsq2.name,), (ones128x1f.name,)],
                           (p3.name,), cc == 0, cc == 7)
                    else:
                        DMA("sync", dst_s.ap()[s, cc, :, t0:t0 + TW], zb[:, cc, :], reads=[(zb.name, cc)])
                if final:
                    TS(mu_r, p2[0:1, :], inv_c, None, ALU.mult, ALU.add, [(p2.name,)], (RK, "mu"))
                    TS(ms_r, p3[0:1, :], inv_c, None, ALU.mult, ALU.add, [(p3.name,)], (RK, "ms"))
                    TT(ve_r, mu_r, mu_r, ALU.mult, [(RK, "mu")], (RK, "ve0"))
                    TT(ve_r, ms_r, ve_r, ALU.subtract, [(RK, "ms"), (RK, "ve0")], (RK, "ve1"))
                    TS(ve_r, ve_r, 1e-5, None, ALU.add, ALU.add, [(RK, "ve1")], (RK, "ve"))
                    newton_rsqrt(RK, ve_r, y_r, n1_r, n2_r, magic[0:1, :])
                    TT(mur_r, mu_r, y_r, ALU.mult, [(RK, "mu"), (RK, "y")], (RK, "mur"))
                    MM(p4[:], ones1x128f[:], y_r, [(RK, "y"), (ones1x128f.name,)], (p4.name,), True, True)
                    MM(p5[:], ones1x128f[:], mur_r, [(RK, "mur"), (ones1x128f.name,)], (p5.name,), True, True)
                    for cc in range(CC):
                        TT(zb[:, cc, :], zb[:, cc, :], p4[:], ALU.mult,
                           [(zb.name, cc), (p4.name,)], (zb.name, cc))
                        TT(zb[:, cc, :], zb[:, cc, :], p5[:], ALU.subtract,
                           [(zb.name, cc), (p5.name,)], (zb.name, cc))
                    for tt in range(4):
                        for cc in range(CC):
                            TR(p6[:, 0:128], zb[:, cc, tt * 128:(tt + 1) * 128], [(zb.name, cc)], (p6.name,))
                            SCOPY(otok[:, tt % 2, cc * 128:(cc + 1) * 128], p6[:, 0:128],
                                  [(p6.name,)], (otok.name, tt % 2))
                        DMA("sync", out_d.ap()[s, t0 + tt * 128:t0 + (tt + 1) * 128, :], otok[:, tt % 2, :],
                            reads=[(otok.name, tt % 2)])
        P.barrier()
        P.emit_phase()
        ph.close()

    if phase_limit >= 1:
        phase_a()
    if phase_limit >= 2:
        phase_b(0, y1_s, x2_s)
    if phase_limit >= 3:
        phase_c1(0, x2_s)
    if phase_limit >= 4:
        phase_c2(0, x2_s, x3_s, final=False)
    if phase_limit >= 5:
        phase_b(1, x3_s, x4_s)
    if phase_limit >= 6:
        phase_c1(1, x4_s)
    if phase_limit >= 7:
        phase_c2(1, x4_s, None, final=True)
    P.barrier()
    P.emit_phase()
    big.close()
    return nc


def _prep_host(inputs):
    f32 = lambda a: np.ascontiguousarray(a, dtype=np.float32)
    g = {}
    g["wkT"] = f32(inputs["L0_attn_Wk"].T)
    g["wvT"] = f32(inputs["L0_attn_Wv"].T)
    g["wrT"] = f32(inputs["L0_attn_Wr"].T)
    g["woT"] = f32(inputs["L0_attn_Wo"].T)
    g["mixk"] = f32(inputs["L0_attn_mix_k"].reshape(-1))
    g["mixv"] = f32(inputs["L0_attn_mix_v"].reshape(-1))
    g["mixr"] = f32(inputs["L0_attn_mix_r"].reshape(-1))
    g["td"] = f32(inputs["L0_attn_time_decay"])
    g["tf"] = f32(inputs["L0_attn_time_first"])
    g["wqkv"] = f32(inputs["L0_ffn_wqkv"])
    g["qs"] = f32(inputs["L0_ffn_q_scale"])
    g["ks"] = f32(inputs["L0_ffn_k_scale"])
    g["woi"] = f32(inputs["L0_ffn_wo"])
    g["lg"] = f32(inputs["L0_ffn_ln_g"])
    g["lb"] = f32(inputs["L0_ffn_ln_b"])
    w1 = np.asarray(inputs["L0_ffn_w1"], np.float32)
    b1 = np.asarray(inputs["L0_ffn_b1"], np.float32)
    w2 = np.asarray(inputs["L0_ffn_w2"], np.float32)
    HID = 2730
    w1p = np.zeros((2, C, 2 * FPAD), np.float32)
    w1p[:, :, 0:HID] = w1[:, :, 0:HID]
    w1p[:, :, FPAD:FPAD + HID] = w1[:, :, HID:2 * HID]
    b1p = np.zeros((2, 2 * FPAD), np.float32)
    b1p[:, 0:HID] = b1[:, 0:HID]
    b1p[:, FPAD:FPAD + HID] = b1[:, HID:2 * HID]
    w2p = np.zeros((2, FPAD, C), np.float32)
    w2p[:, 0:HID, :] = w2
    g["w1p"] = np.ascontiguousarray(w1p)
    g["b1p"] = np.ascontiguousarray(b1p)
    g["w2p"] = np.ascontiguousarray(w2p)
    g["b2"] = f32(inputs["L0_ffn_b2"])
    return g


def kernel(**inputs):
    x = np.ascontiguousarray(np.asarray(inputs["L0_attn_x"], np.float32))
    g = _prep_host(inputs)
    nc = build_program(taps=tuple(DEBUG_TAPS), phase_limit=PHASE_LIMIT)
    in_maps = []
    for c in range(NC_CORES):
        m = dict(g)
        m["x"] = x[c * S:(c + 1) * S]
        in_maps.append(m)
    res = run_bass_kernel_spmd(nc, in_maps, core_ids=list(range(NC_CORES)))
    if DEBUG_TAPS:
        return res
    out = np.concatenate([res.results[c]["out"] for c in range(NC_CORES)], axis=0)
    return out.astype(np.float32)


NC_CORES = 8
